# version 01
# baseline (speedup 1.0000x reference)
import math
import numpy as np

import concourse.bass as bass
import concourse.mybir as mybir
from concourse.alu_op_type import AluOpType

F32 = mybir.dt.float32
BF16 = mybir.dt.bfloat16
AF = mybir.ActivationFunctionType
AX = mybir.AxisListType.X

D = 4096
S = 2048
DBLK = 32
CH = 4
NT = 16
H = 32
QLAT = 1024
EPS_LN = 1e-5
EPS_RMS = 1e-6
M_SCALE = 0.1 * 1.0 * math.log(128.0) + 1.0
SOFTMAX_SCALE = (128.0 ** -0.5) * M_SCALE * M_SCALE
NEG_BIG = -1e30

A_CK = 0
A_KP = 4096
A_COS = 6144
A_SIN = 10240
A_QF = 14336
A_KF = 16384
A_V = 18432
A_PR = 20480
A_PT = 22528
A_WQB = 30720
A_COS2 = 33536
A_SIN2 = 37632
A_KVK = 32768
A_KVV = 33024

D_SLAB = 0
D_MIDG = 8192
D_SQ = 10240
D_GMAP = 14336
D_GTT = 14848
D_SPARE = 15360

C_XF = 0
C_X1 = 1024
C_LGD = 2048
C_LGT = 2560
C_U = 2568
C_L2 = 2576
C_MSK = 2584
C_W = 2592
C_M1 = 2600
C_M2 = 2601
C_SUM = 2602
C_RS = 2603
C_PEN = 2608
C_ROWS = 3072


class Ctr:
    __slots__ = ("sem", "n", "name")

    def __init__(self, sem, name):
        self.sem = sem
        self.n = 0
        self.name = name

    def inc(self, instr, k=1):
        instr.then_inc(self.sem, k)
        self.n += k
        return self.n


class NS:
    pass


class DmaPipe:

    def __init__(self, ctrs):
        self.ctrs = ctrs
        self.n = len(ctrs)
        self.rec = []

    def issue(self, q, out, in_, k=1):
        slot = len(self.rec) % self.n
        c = self.ctrs[slot]
        if k == 1:
            c.inc(q.dma_start(out=out, in_=in_), 16)
        else:
            for o, i in zip(out, in_):
                c.inc(q.dma_start(out=o, in_=i), 16)
        self.rec.append((slot, c.n))
        return len(self.rec) - 1

    def wait(self, eng, idx):
        if idx < 0:
            return
        slot, val = self.rec[idx]
        eng.wait_ge(self.ctrs[slot].sem, val)


SEMS = ["sA", "sA2", "sW", "sW2", "sWf", "sX", "sX2", "sSq", "sSqf", "sStat",
        "sFin", "sBc", "sBce", "sYa", "sMM", "sEv", "sVe", "sRES", "sXW", "sXW2",
        "sG", "sCtx", "sCtx2", "sVs", "sSc", "sMM2", "sO", "sAh", "sSP", "sSP2",
        "sWh", "sWh2"]


def build(nc, es, debug_taps=(), dev_expose=False):
    ns = NS()
    ns.nc = nc
    ns.dev_expose = dev_expose
    ns.x0t = nc.dram_tensor("x0t", [D, S], F32, kind="ExternalInput").ap()
    shapes = dict(fw_base=[32, 128, 4096], fw_qa=[8, 128, 4096], fw_qb=[32, 128, 1024],
                  fw_kva=[3, 128, 4096], fw_kvk=[32, 128, 128], fw_kvv=[32, 128, 256],
                  fw_wo=[32, 128, 4096], fw_rt=[128, 256], fw_la=[128, 4096],
                  fw_lb=[128, 4096], gq=[128, 8], gkv=[128, 2], cos_t=[64, 2048],
                  sin_t=[64, 2048], ident=[128, 128], p64t=[64, 64], tri=[128, 128],
                  ones_row=[1, 128], r2=[8, 128])
    ns.dr = {k: nc.dram_tensor(k, s, F32, kind="ExternalInput").ap() for k, s in shapes.items()}
    ik = "ExternalOutput" if dev_expose else "Internal"
    ns.x1t = nc.dram_tensor("x1t", [D, S], F32, kind=ik).ap()
    ns.x2t = nc.dram_tensor("x2t", [D, S], F32, kind=ik).ap()
    ns.basebf = nc.dram_tensor("basebf", [32, 128, 4096], BF16, kind="Internal").ap()
    ns.ckv_scr = nc.dram_tensor("ckv_scr", [128, 6144], BF16, kind=ik).ap()
    ns.ctx_scr = nc.dram_tensor("ctx_scr", [H, 128, S], BF16, kind=ik).ap()
    ns.x3t = nc.dram_tensor("x3t", [D, S], F32, kind="ExternalOutput").ap()
    dbg_shapes = {"y0": [D, S], "cq": [QLAT, S], "ck": [128, 4096],
                  "kp": [64, S], "qh": [128, S], "kh": [128, S], "vh": [128, NT * 128],
                  "ctxh": [128, S], "y2": [D, S], "probs": [128, S],
                  "midg": [128, S], "gmap": [128, 512]}
    ns.dbg = {t: nc.dram_tensor("dbg_" + t, dbg_shapes[t], F32, kind="ExternalOutput").ap()
              for t in debug_taps}

    sb = lambda name, shape, dt: es.enter_context(nc.sbuf_tensor(name, shape, dt))
    ps = lambda name, shape: es.enter_context(nc.psum_tensor(name, shape, F32))
    ns.A = sb("A", [128, 65536], BF16)
    ns.Dn = sb("Dn", [128, 16384], BF16)
    ns.Cf = sb("Cf", [128, 2048], F32)
    ns.Ef = sb("Ef", [128, 8192], BF16)
    ns.Ff = sb("Ff", [128, 4096], BF16)
    ns.lgf = sb("lgf", [128, 576], F32)
    ns.ident_bf = sb("ident_bf", [128, 128], BF16)
    ns.ident8_f32 = sb("ident8_f32", [8, 8], F32)
    ns.p64_bf = sb("p64_bf", [64, 64], BF16)
    ns.p64_hi = sb("p64_hi", [128, 64], BF16)
    ns.tri_sb = sb("tri_sb", [128, 128], BF16)
    ns.ones_bf = sb("ones_bf", [128, 1], BF16)
    ns.ones_f32 = sb("ones_f32", [1, 128], F32)
    ns.r2_bf = sb("r2_bf", [8, 128], BF16)
    ns.rt_sb = sb("rt_sb", [128, 256], BF16)
    ns.gq_sb = sb("gq_sb", [128, 8], F32)
    ns.gkv_sb = sb("gkv_sb", [128, 2], F32)
    ns.stat_m = sb("stat_m", [1, 512], F32)
    ns.stat_r = sb("stat_r", [1, 512], F32)
    ns.nm_sb = sb("nm_sb", [128, 1], F32)
    ns.den_sb = sb("den_sb", [128, 1], F32)
    ns.rden_sb = sb("rden_sb", [128, 1], F32)
    ns.mrbf = sb("mrbf", [128, 1536], BF16)
    ns.stg = sb("stg", [128, 1024], BF16)
    ns.ctxstg = sb("ctxstg", [128, 1024], BF16)
    ns.row1 = ns.stg[0:1, 0:1024].bitcast(F32)
    ns.row2 = ns.ctxstg[0:1, 0:1024].bitcast(F32)

    ns.scores = ps("scores", [128, 2048])
    ns.pbA = ps("pbA", [128, 512])
    ns.pbB = ps("pbB", [128, 512])
    ns.pbC = ps("pbC", [128, 512])
    ns.pbD = ps("pbD", [128, 512])
    ns.pb = [ns.pbA, ns.pbB, ns.pbC, ns.pbD]

    for name in SEMS:
        setattr(ns, name, Ctr(es.enter_context(nc.semaphore(name)), name))
    ns.all_ctrs = [getattr(ns, n) for n in SEMS]
    ns.pA = DmaPipe([ns.sA, ns.sA2])
    ns.pW = DmaPipe([ns.sW, ns.sW2])
    ns.pWh = DmaPipe([ns.sWh, ns.sWh2])
    ns.pX = DmaPipe([ns.sX, ns.sX2])
    ns.pXW = DmaPipe([ns.sXW, ns.sXW2])
    ns.pSP = DmaPipe([ns.sSP, ns.sSP2])
    ns.pCtx = DmaPipe([ns.sCtx, ns.sCtx2])
    return ns


def snap(ns):
    return {c.name: c.n for c in ns.all_ctrs}


def wait_pre(ns, eng, pre):
    for c in ns.all_ctrs:
        if pre[c.name] > 0:
            eng.wait_ge(c.sem, pre[c.name])


def emit_consts(ns, blk):
    dr = ns.dr
    pre = snap(ns)

    def g(gp):
        for dst, src in [(ns.ident_bf[:], dr["ident"]), (ns.p64_bf[:], dr["p64t"]),
                         (ns.p64_hi[64:128, :], dr["p64t"]),
                         (ns.r2_bf[:], dr["r2"]), (ns.rt_sb[:], dr["fw_rt"]),
                         (ns.Ef[:, 0:4096], dr["fw_la"]), (ns.Ef[:, 4096:8192], dr["fw_lb"]),
                         (ns.tri_sb[:], dr["tri"])]:
            ns.sO.inc(gp.dma_start(out=dst, in_=src), 16)

    def s(sy):
        for dst, src in [(ns.ident8_f32[:], dr["ident"][0:8, 0:8]),
                         (ns.ones_f32[:], dr["ones_row"]),
                         (ns.gq_sb[:], dr["gq"]), (ns.gkv_sb[:], dr["gkv"])]:
            ns.sAh.inc(sy.dma_start(out=dst, in_=src), 16)

    def v(ve):
        ns.sVe.inc(ve.memset(ns.ones_bf[:], 1.0))

    blk.gpsimd(g)
    blk.sync(s)
    blk.vector(v)


def emit_stats(ns, blk, src_dram):
    pre = snap(ns)
    A, Dn = ns.A, ns.Dn
    sq0, sqf0 = ns.sSq.n, ns.sSqf.n
    pa0 = len(ns.pA.rec)

    def g(gp):
        wait_pre(ns, gp, pre)
        for d in range(DBLK):
            if d >= 2:
                gp.wait_ge(ns.sSq.sem, sq0 + d - 1)
            ns.pA.issue(gp, A[:, d * 2048:(d + 1) * 2048],
                        src_dram[d * 128:(d + 1) * 128, :])

    def v(ve):
        wait_pre(ns, ve, pre)
        for d in range(DBLK):
            ns.pA.wait(ve, pa0 + d)
            if d >= 2:
                ve.wait_ge(ns.sSqf.sem, sqf0 + (d - 1))
            slot = D_SQ + (d % 2) * 2048
            ns.sSq.inc(ve.tensor_tensor(Dn[:, slot:slot + 2048],
                                        A[:, d * 2048:(d + 1) * 2048],
                                        A[:, d * 2048:(d + 1) * 2048], op=AluOpType.mult))

    def t(te):
        wait_pre(ns, te, pre)
        for d in range(DBLK):
            te.wait_ge(ns.sSq.sem, sq0 + (d + 1))
            slot = D_SQ + (d % 2) * 2048
            for c in range(CH):
                te.matmul(ns.scores[0:1, c * 512:(c + 1) * 512], ns.ones_bf[:],
                          A[:, d * 2048 + c * 512: d * 2048 + (c + 1) * 512],
                          start=(d == 0), stop=(d == DBLK - 1))
                last = te.matmul(ns.scores[64:65, c * 512:(c + 1) * 512], ns.ones_bf[:],
                                 Dn[:, slot + c * 512: slot + (c + 1) * 512],
                                 start=False, stop=(d == DBLK - 1))
            if d < DBLK - 1:
                ns.sSqf.inc(last)
            else:
                ns.sStat.inc(last)

    blk.gpsimd(g)
    blk.vector(v)
    blk.tensor(t)


def emit_apply(ns, blk, half):
    pre = snap(ns)
    A = ns.A
    bias_c = math.log(half)
    st0 = ns.sStat.n
    bc0, bce0, ya0, ve0, fin0 = ns.sBc.n, ns.sBce.n, ns.sYa.n, ns.sVe.n, ns.sFin.n

    def v(ve):
        wait_pre(ns, ve, pre)
        ve.wait_ge(ns.sStat.sem, st0)
        for c in range(CH):
            cs = slice(c * 512, (c + 1) * 512)
            if c > 0:
                ve.wait_ge(ns.sBc.sem, bc0 + 2 * c)
            ve.tensor_scalar_mul(ns.stat_m[:], ns.scores[0:1, cs], 1.0 / D)
            ve.tensor_scalar_mul(ns.row1, ns.scores[64:65, cs], 1.0 / D)
            ve.tensor_tensor(ns.row2, ns.stat_m[:], ns.stat_m[:], op=AluOpType.mult)
            ve.tensor_tensor(ns.row1, ns.row1, ns.row2, op=AluOpType.subtract)
            ns.sVe.inc(ve.tensor_scalar(ns.row1, ns.row1, EPS_LN, 1.0 / (half * half),
                                        op0=AluOpType.add, op1=AluOpType.mult))
            ve.wait_ge(ns.sBce.sem, bce0 + 2 * (c + 1))
            i = None
            for d in range(DBLK):
                off = d * 2048 + c * 512
                ve.tensor_tensor(ns.Dn[:, D_SPARE:D_SPARE + 512], A[:, off:off + 512],
                                 ns.mrbf[:, 0:512], op=AluOpType.subtract)
                i = ve.tensor_tensor(A[:, off:off + 512], ns.Dn[:, D_SPARE:D_SPARE + 512],
                                     ns.mrbf[:, 512:1024], op=AluOpType.mult)
            ns.sYa.inc(i)

    def sc(se):
        wait_pre(ns, se, pre)
        for c in range(CH):
            se.wait_ge(ns.sVe.sem, ve0 + c + 1)
            se.activation(ns.row2, ns.row1, AF.Ln, bias=0.0, scale=1.0)
            ns.sFin.inc(se.activation(ns.stat_r[:], ns.row2, AF.Exp, bias=0.0, scale=-0.5))
            se.wait_ge(ns.sBc.sem, bc0 + 2 * (c + 1))
            if c > 0:
                se.wait_ge(ns.sYa.sem, ya0 + c)
            se.copy(ns.mrbf[:, 0:512], ns.pbA[:])
            ns.sBce.inc(se.copy(ns.mrbf[:, 512:1024], ns.pbB[:]), 2)

    def t(te):
        wait_pre(ns, te, pre)
        for c in range(CH):
            te.wait_ge(ns.sFin.sem, fin0 + c + 1)
            if c > 0:
                te.wait_ge(ns.sBce.sem, bce0 + 2 * c)
            te.matmul(ns.pbA[:], ns.ones_f32[:], ns.stat_m[:], start=True, stop=True)
            ns.sBc.inc(te.matmul(ns.pbB[:], ns.ones_f32[:], ns.stat_r[:],
                                 start=True, stop=True), 2)

    blk.vector(v)
    blk.scalar(sc)
    blk.tensor(t)
    return {"ya": [ya0 + c + 1 for c in range(CH)]}


def emit_ffn(ns, blk, src_x, dst_x, ya_tgt, use_basebf, cache_basebf, tap=None):
    pre = snap(ns)
    A, Dn, Cf, Ef, Ff = ns.A, ns.Dn, ns.Cf, ns.Ef, ns.Ff
    nc = ns.nc

    mm0, sc0, ve0, vs0, g0 = ns.sMM.n, ns.sSc.n, ns.sVe.n, ns.sVs.n, ns.sG.n

    def t(te):
        wait_pre(ns, te, pre)
        for c in range(CH):
            te.wait_ge(ns.sYa.sem, ya_tgt[c])
            for k in range(DBLK):
                i = te.matmul(ns.pbC[0:8, 0:512], ns.rt_sb[:, k * 8:(k + 1) * 8],
                              A[:, k * 2048 + c * 512: k * 2048 + c * 512 + 512],
                              start=(k == 0), stop=(k == DBLK - 1))
            ns.sMM.inc(i)
            te.wait_ge(ns.sSc.sem, sc0 + 3 * c + 1)
            for tt in range(4):
                i = te.matmul(ns.pbD[:, tt * 8:(tt + 1) * 8],
                              ns.lgf[0:8, tt * 128:(tt + 1) * 128],
                              ns.ident8_f32[:], start=True, stop=True)
            ns.sMM.inc(i)
            te.wait_ge(ns.sG.sem, g0 + 8 * c + 8)
            for tt in range(4):
                i = te.matmul(ns.pbA[0:8, tt * 128:(tt + 1) * 128],
                              ns.mrbf[:, 1024 + tt * 8: 1024 + (tt + 1) * 8],
                              ns.ident_bf[:], start=True, stop=True)
            te.wait_ge(ns.sSc.sem, sc0 + 3 * c + 2)
            ns.sMM.inc(te.matmul(ns.pbB[:, 0:512], ns.r2_bf[:],
                                 Dn[0:8, D_GTT:D_GTT + 512], start=True, stop=True))
            for k in range(DBLK):
                i = te.matmul(ns.pbC[:, 0:512], Ef[:, k * 128:(k + 1) * 128],
                              A[:, k * 2048 + c * 512: k * 2048 + c * 512 + 512],
                              start=(k == 0), stop=(k == DBLK - 1))
            ns.sMM.inc(i)

    def sc(se):
        wait_pre(ns, se, pre)
        for c in range(CH):
            se.wait_ge(ns.sMM.sem, mm0 + 4 * c + 1)
            ns.sSc.inc(se.activation(ns.lgf[0:8, 0:512], ns.pbC[0:8, 0:512],
                                     AF.Copy, bias=0.0, scale=2.0))
            for tt in range(4):
                se.wait_ge(ns.sVs.sem, vs0 + (4 * c + tt) * 12 + 2)
                ns.sG.inc(se.activation(ns.lgf[:, 520:528], ns.lgf[:, 512:520],
                                        AF.Exp, bias=ns.nm_sb[:, 0:1], scale=1.0))
                se.wait_ge(ns.sG.sem, g0 + 8 * c + 2 * tt + 2)
            se.wait_ge(ns.sMM2.sem, ns._ffn_mm2_base + c + 1)
            ns.sSc.inc(se.copy(Dn[0:8, D_GTT:D_GTT + 512], ns.pbA[0:8, 0:512]))
            se.wait_ge(ns.sMM.sem, mm0 + 4 * c + 3)
            ns.sSc.inc(se.copy(Dn[:, D_GMAP:D_GMAP + 512], ns.pbB[:, 0:512]))

    def v(ve):
        wait_pre(ns, ve, pre)
        for c in range(CH):
            for tt in range(4):
                base = vs0 + (4 * c + tt) * 12

                def step(instr, k):
                    ns.sVs.inc(instr)
                    assert ns.sVs.n == base + k, (ns.sVs.n, base, k)
                    ve.wait_ge(ns.sVs.sem, base + k)

                ve.wait_ge(ns.sMM.sem, mm0 + 4 * c + 2)
                step(ve.tensor_copy(ns.lgf[:, 512:520], ns.pbD[:, tt * 8:(tt + 1) * 8]), 1)
                step(ve.reduce_max(ns.nm_sb[:], ns.lgf[:, 512:520], axis=AX, negate=True), 2)
                step(ve.tensor_scalar(ns.lgf[:, 528:536], ns.lgf[:, 512:520],
                                      ns.nm_sb[:, 0:1], None, op0=AluOpType.add), 3)
                step(ve.tensor_scalar(ns.lgf[:, 536:544], ns.lgf[:, 528:536], 0.0, None,
                                      op0=AluOpType.is_ge), 4)
                step(ve.tensor_scalar_mul(ns.lgf[:, 552:560], ns.lgf[:, 536:544], NEG_BIG), 5)
                step(ve.tensor_tensor(ns.lgf[:, 528:536], ns.lgf[:, 512:520],
                                      ns.lgf[:, 552:560], op=AluOpType.add), 6)
                step(ve.reduce_max(ns.lgf[:, 560:561], ns.lgf[:, 528:536],
                                   axis=AX, negate=True), 7)
                step(ve.tensor_scalar(ns.lgf[:, 536:544], ns.lgf[:, 512:520],
                                      ns.lgf[:, 560:561], None, op0=AluOpType.add), 8)
                step(ve.tensor_scalar(ns.lgf[:, 536:544], ns.lgf[:, 536:544], 0.0, None,
                                      op0=AluOpType.is_ge), 9)
                ve.wait_ge(ns.sG.sem, g0 + 8 * c + 2 * tt + 1)
                step(ve.tensor_tensor(ns.lgf[:, 544:552], ns.lgf[:, 520:528],
                                      ns.lgf[:, 536:544], op=AluOpType.mult), 10)
                step(ve.reduce_sum(ns.lgf[:, 561:562], ns.lgf[:, 544:552], axis=AX), 11)
                step(ve.reciprocal(ns.lgf[:, 562:563], ns.lgf[:, 561:562]), 12)
                ns.sG.inc(ve.tensor_scalar(ns.mrbf[:, 1024 + tt * 8:1024 + (tt + 1) * 8],
                                           ns.lgf[:, 544:552], ns.lgf[:, 562:563], None,
                                           op0=AluOpType.mult))
            ve.wait_ge(ns.sMM.sem, mm0 + 4 * c + 4)
            ve.wait_ge(ns.sSc.sem, sc0 + 3 * c + 3)
            ns.sVe.inc(ve.tensor_tensor(Dn[:, D_MIDG + c * 512: D_MIDG + (c + 1) * 512],
                                        ns.pbC[:, 0:512], Dn[:, D_GMAP:D_GMAP + 512],
                                        op=AluOpType.mult))

    ns._ffn_mm2_base = ns.sMM2.n

    def t2_mark(te):
        pass

    def t_full(te):
        wait_pre(ns, te, pre)
        for c in range(CH):
            te.wait_ge(ns.sYa.sem, ya_tgt[c])
            if c > 0:
                te.wait_ge(ns.sVe.sem, ve0 + c)
            for k in range(DBLK):
                i = te.matmul(ns.pbC[0:8, 0:512], ns.rt_sb[:, k * 8:(k + 1) * 8],
                              A[:, k * 2048 + c * 512: k * 2048 + c * 512 + 512],
                              start=(k == 0), stop=(k == DBLK - 1))
            ns.sMM.inc(i)
            te.wait_ge(ns.sSc.sem, sc0 + 3 * c + 1)
            for tt in range(4):
                i = te.matmul(ns.pbD[:, tt * 8:(tt + 1) * 8],
                              ns.lgf[0:8, tt * 128:(tt + 1) * 128],
                              ns.ident8_f32[:], start=True, stop=True)
            ns.sMM.inc(i)
            te.wait_ge(ns.sG.sem, g0 + 8 * c + 8)
            for tt in range(4):
                i = te.matmul(ns.pbA[0:8, tt * 128:(tt + 1) * 128],
                              ns.mrbf[:, 1024 + tt * 8: 1024 + (tt + 1) * 8],
                              ns.ident_bf[:], start=True, stop=True)
            ns.sMM2.inc(i)
            te.wait_ge(ns.sSc.sem, sc0 + 3 * c + 2)
            ns.sMM.inc(te.matmul(ns.pbB[:, 0:512], ns.r2_bf[:],
                                 Dn[0:8, D_GTT:D_GTT + 512], start=True, stop=True))
            for k in range(DBLK):
                i = te.matmul(ns.pbC[:, 0:512], Ef[:, k * 128:(k + 1) * 128],
                              A[:, k * 2048 + c * 512: k * 2048 + c * 512 + 512],
                              start=(k == 0), stop=(k == DBLK - 1))
            ns.sMM.inc(i)

    blk.tensor(t_full)
    blk.scalar(sc)
    blk.vector(v)
    midg_tgt = ns.sVe.n

    if tap == "midg":
        _tap_bf16(ns, blk, ns.Dn, D_MIDG, ns.dbg["midg"], 2048)
    if tap == "gmap":
        _tap_bf16(ns, blk, ns.Dn, D_GMAP, ns.dbg["gmap"], 512)

    pre2 = snap(ns)
    wf0, mmc0, res0 = ns.sWf.n, ns.sMM.n, ns.sRES.n
    slabp = ns.pWh if use_basebf else ns.pW
    pw0, px0, pxw0 = len(slabp.rec), len(ns.pX.rec), len(ns.pXW.rec)

    def g2(gp):
        wait_pre(ns, gp, pre2)
        if use_basebf:
            return
        for ob in range(DBLK):
            for half in range(2):
                si = 2 * ob + half
                if si >= 2:
                    gp.wait_ge(ns.sWf.sem, wf0 + si - 1)
                slabp.issue(gp, Ff[:, (si % 2) * 2048:((si % 2) + 1) * 2048],
                            ns.dr["fw_base"][ob, :, half * 2048:(half + 1) * 2048])
                if cache_basebf:
                    slabp.wait(gp, pw0 + si)
                    ns.sO.inc(gp.dma_start(out=ns.basebf[ob, :, half * 2048:(half + 1) * 2048],
                                           in_=Ff[:, (si % 2) * 2048:((si % 2) + 1) * 2048]), 16)

    def s2w(se):
        wait_pre(ns, se, pre2)
        for ob in range(DBLK):
            for half in range(2):
                si = 2 * ob + half
                if si >= 2:
                    se.wait_ge(ns.sWf.sem, wf0 + si - 1)
                slabp.issue(se, Ff[:, (si % 2) * 2048:((si % 2) + 1) * 2048],
                            ns.basebf[ob, :, half * 2048:(half + 1) * 2048])

    def s2(sy):
        wait_pre(ns, sy, pre2)
        for ob in range(DBLK):
            for c in range(CH):
                xi = ob * CH + c
                if xi >= 2:
                    sy.wait_ge(ns.sRES.sem, res0 + xi - 1)
                ns.pX.issue(sy, Cf[:, C_XF + (xi % 2) * 512: C_XF + ((xi % 2) + 1) * 512],
                            src_x[ob * 128:(ob + 1) * 128, c * 512:(c + 1) * 512])
            for c in range(CH):
                xi = ob * CH + c
                sy.wait_ge(ns.sRES.sem, res0 + xi + 1)
                ns.pXW.issue(sy, dst_x[ob * 128:(ob + 1) * 128, c * 512:(c + 1) * 512],
                             Cf[:, C_X1 + (xi % 2) * 512: C_X1 + ((xi % 2) + 1) * 512])

    def t2(te):
        wait_pre(ns, te, pre2)
        te.wait_ge(ns.sVe.sem, midg_tgt)
        for ob in range(DBLK):
            banks = ns.pb if ob % 2 == 0 else [ns.scores[:, i * 512:(i + 1) * 512] for i in range(4)]
            if ob >= 2:
                te.wait_ge(ns.sRES.sem, res0 + (ob - 2) * CH + CH)
            for half in range(2):
                si = 2 * ob + half
                slabp.wait(te, pw0 + si)
                for c in range(CH):
                    for k in range(16):
                        dglob = half * 16 + k
                        i = te.matmul(banks[c] if ob % 2 else banks[c][:, 0:512],
                                      Ff[:, (si % 2) * 2048 + k * 128:(si % 2) * 2048 + (k + 1) * 128],
                                      A[:, dglob * 2048 + c * 512: dglob * 2048 + c * 512 + 512],
                                      start=(half == 0 and k == 0), stop=False)
                ns.sWf.inc(i)
            for c in range(CH):
                i = te.matmul(banks[c] if ob % 2 else banks[c][:, 0:512],
                              Ef[:, 4096 + ob * 128: 4096 + (ob + 1) * 128],
                              Dn[:, D_MIDG + c * 512: D_MIDG + (c + 1) * 512],
                              start=False, stop=True)
                ns.sMM.inc(i)

    def v2(ve):
        wait_pre(ns, ve, pre2)
        for ob in range(DBLK):
            banks = ns.pb if ob % 2 == 0 else [ns.scores[:, i * 512:(i + 1) * 512] for i in range(4)]
            for c in range(CH):
                xi = ob * CH + c
                ve.wait_ge(ns.sMM.sem, mmc0 + ob * CH + c + 1)
                ns.pX.wait(ve, px0 + xi)
                ns.pXW.wait(ve, pxw0 + xi - 2)
                src = banks[c] if ob % 2 else banks[c][:, 0:512]
                ns.sRES.inc(ve.tensor_tensor(
                    Cf[:, C_X1 + (xi % 2) * 512: C_X1 + ((xi % 2) + 1) * 512],
                    Cf[:, C_XF + (xi % 2) * 512: C_XF + ((xi % 2) + 1) * 512],
                    src, op=AluOpType.add))

    if use_basebf:
        blk.scalar(s2w)
        blk.sync(s2)
    else:
        blk.gpsimd(g2)
        blk.sync(s2)
    blk.tensor(t2)
    blk.vector(v2)
    return {"xw_done": ns.sXW.n}


def _tap_bf16(ns, blk, buf, off, dst_dram, width, rows=128):
    pre = snap(ns)

    def g(gp):
        wait_pre(ns, gp, pre)
        ns.sO.inc(gp.dma_start(out=dst_dram, in_=buf[0:rows, off:off + width]), 16)

    blk.gpsimd(g)


def _tap_barrier(ns, blk):
    pre = snap(ns)
    for name in ["gpsimd", "sync", "tensor", "scalar", "vector"]:
        getattr(blk, name)(lambda e: wait_pre(ns, e, pre))


def emit_attnprep(ns, blk, ya_tgt):
    pre = snap(ns)
    A, Dn, Cf, Ff = ns.A, ns.Dn, ns.Cf, ns.Ff
    w0, wf0, mm0, ev0 = ns.sW.n, ns.sWf.n, ns.sMM.n, ns.sEv.n
    pw0 = len(ns.pW.rec)

    def g(gp):
        wait_pre(ns, gp, pre)
        for ob in range(8):
            for half in range(2):
                si = 2 * ob + half
                if ob >= 1:
                    if half == 0:
                        gp.wait_ge(ns.sWf.sem, wf0 + ob)
                    else:
                        gp.wait_ge(ns.sMM.sem, mm0 + ob)
                ns.pW.issue(gp, Ff[:, (si % 2) * 2048:((si % 2) + 1) * 2048],
                            ns.dr["fw_qa"][ob, :, half * 2048:(half + 1) * 2048])

    def t(te):
        wait_pre(ns, te, pre)
        for c in range(CH):
            te.wait_ge(ns.sYa.sem, ya_tgt[c])
        for ob in range(8):
            banks = ns.pb if ob % 2 == 0 else [ns.scores[:, i * 512:(i + 1) * 512] for i in range(4)]
            if ob >= 2:
                te.wait_ge(ns.sEv.sem, ev0 + ob - 1)
            for half in range(2):
                si = 2 * ob + half
                ns.pW.wait(te, pw0 + si)
                for c in range(CH):
                    for k in range(16):
                        dglob = half * 16 + k
                        i = te.matmul(banks[c][:, 0:512] if ob % 2 == 0 else banks[c],
                                      Ff[:, (si % 2) * 2048 + k * 128:(si % 2) * 2048 + (k + 1) * 128],
                                      A[:, dglob * 2048 + c * 512: dglob * 2048 + c * 512 + 512],
                                      start=(half == 0 and k == 0), stop=(half == 1 and k == 15))
                if half == 0:
                    ns.sWf.inc(i)
                else:
                    ns.sMM.inc(i)

    def sc(se):
        wait_pre(ns, se, pre)
        for ob in range(8):
            banks = ns.pb if ob % 2 == 0 else [ns.scores[:, i * 512:(i + 1) * 512] for i in range(4)]
            se.wait_ge(ns.sMM.sem, mm0 + ob + 1)
            for c in range(CH):
                i = se.copy(Dn[:, ob * 2048 + c * 512: ob * 2048 + (c + 1) * 512],
                            banks[c][:, 0:512] if ob % 2 == 0 else banks[c])
            ns.sEv.inc(i)

    blk.gpsimd(g)
    blk.tensor(t)
    blk.scalar(sc)

    pre2 = snap(ns)
    sq0, sqf0, st0 = ns.sSq.n, ns.sSqf.n, ns.sStat.n
    ve0, fin0, bc0, bce0, ya20 = ns.sVe.n, ns.sFin.n, ns.sBc.n, ns.sBce.n, ns.sYa.n

    def v2(ve):
        wait_pre(ns, ve, pre2)
        for ob in range(8):
            if ob >= 1:
                ve.wait_ge(ns.sSqf.sem, sqf0 + ob)
            ns.sSq.inc(ve.tensor_tensor(Ff[:, 0:2048], Dn[:, ob * 2048:(ob + 1) * 2048],
                                        Dn[:, ob * 2048:(ob + 1) * 2048], op=AluOpType.mult))

    def t2(te):
        wait_pre(ns, te, pre2)
        for ob in range(8):
            te.wait_ge(ns.sSq.sem, sq0 + ob + 1)
            for c in range(CH):
                i = te.matmul(ns.scores[0:1, c * 512:(c + 1) * 512], ns.ones_bf[:],
                              Ff[:, c * 512:(c + 1) * 512],
                              start=(ob == 0), stop=(ob == 7))
            if ob < 7:
                ns.sSqf.inc(i)
            else:
                ns.sStat.inc(i)
        for c in range(CH):
            te.wait_ge(ns.sFin.sem, fin0 + c + 1)
            if c > 0:
                te.wait_ge(ns.sBce.sem, bce0 + c)
            ns.sBc.inc(te.matmul(ns.pbA[:], ns.ones_f32[:], ns.stat_r[:], start=True, stop=True))

    def v3(ve):
        for c in range(CH):
            cs = slice(c * 512, (c + 1) * 512)
            if c == 0:
                ve.wait_ge(ns.sStat.sem, st0 + 1)
            else:
                ve.wait_ge(ns.sBc.sem, bc0 + c)
            ve.tensor_scalar_mul(ns.row1, ns.scores[0:1, cs], 1.0 / QLAT)
            ns.sVe.inc(ve.tensor_scalar(ns.row1, ns.row1, EPS_RMS,
                                        1.0 / (SOFTMAX_SCALE * SOFTMAX_SCALE),
                                        op0=AluOpType.add, op1=AluOpType.mult))
            ve.wait_ge(ns.sBce.sem, bce0 + c + 1)
            i = None
            for ob in range(8):
                off = ob * 2048 + c * 512
                ve.tensor_scalar(Ff[:, 2048:2560], Dn[:, off:off + 512],
                                 ns.gq_sb[:, ob:ob + 1], None, op0=AluOpType.mult)
                i = ve.tensor_tensor(Dn[:, off:off + 512], Ff[:, 2048:2560],
                                     ns.mrbf[:, 512:1024], op=AluOpType.mult)
            ns.sYa.inc(i)

    def sc3(se):
        wait_pre(ns, se, pre2)
        for c in range(CH):
            se.wait_ge(ns.sVe.sem, ve0 + c + 1)
            if c > 0:
                se.wait_ge(ns.sBc.sem, bc0 + c)
            se.activation(ns.row2, ns.row1, AF.Ln, bias=0.0, scale=1.0)
            ns.sFin.inc(se.activation(ns.stat_r[:], ns.row2, AF.Exp, bias=0.0, scale=-0.5))
            se.wait_ge(ns.sBc.sem, bc0 + c + 1)
            if c > 0:
                se.wait_ge(ns.sYa.sem, ya20 + c)
            ns.sBce.inc(se.copy(ns.mrbf[:, 512:1024], ns.pbA[:]))

    blk.vector(v2)
    blk.tensor(t2)
    blk.scalar(sc3)
    blk.vector(v3)

    pre4 = snap(ns)
    w40, wf40, mm40, ev40 = ns.sW.n, ns.sWf.n, ns.sMM.n, ns.sEv.n
    pw40, psp0 = len(ns.pW.rec), len(ns.pSP.rec)

    def g4(gp):
        wait_pre(ns, gp, pre4)
        for ob3 in range(3):
            for half in range(2):
                si = 2 * ob3 + half
                if ob3 >= 1:
                    if half == 0:
                        gp.wait_ge(ns.sWf.sem, wf40 + ob3)
                    else:
                        gp.wait_ge(ns.sMM.sem, mm40 + ob3)
                ns.pW.issue(gp, Ff[:, (si % 2) * 2048:((si % 2) + 1) * 2048],
                            ns.dr["fw_kva"][ob3, :, half * 2048:(half + 1) * 2048])

    def t4(te):
        wait_pre(ns, te, pre4)
        for ob3 in range(3):
            banks = ns.pb if ob3 % 2 == 0 else [ns.scores[:, i * 512:(i + 1) * 512] for i in range(4)]
            if ob3 == 2:
                te.wait_ge(ns.sEv.sem, ev40 + 4)
            for half in range(2):
                si = 2 * ob3 + half
                ns.pW.wait(te, pw40 + si)
                for c in range(CH):
                    for k in range(16):
                        dglob = half * 16 + k
                        i = te.matmul(banks[c][:, 0:512] if ob3 % 2 == 0 else banks[c],
                                      Ff[:, (si % 2) * 2048 + k * 128:(si % 2) * 2048 + (k + 1) * 128],
                                      A[:, dglob * 2048 + c * 512: dglob * 2048 + c * 512 + 512],
                                      start=(half == 0 and k == 0), stop=(half == 1 and k == 15))
                if half == 0:
                    ns.sWf.inc(i)
                else:
                    ns.sMM.inc(i)

    def sc5(se):
        wait_pre(ns, se, pre4)
        for ob3 in range(3):
            banks = ns.pb if ob3 % 2 == 0 else [ns.scores[:, i * 512:(i + 1) * 512] for i in range(4)]
            se.wait_ge(ns.sMM.sem, mm40 + ob3 + 1)
            for c in range(CH):
                j = ob3 * 4 + c
                ns.pSP.wait(se, psp0 + j - 2)
                src = banks[c][:, 0:512] if ob3 % 2 == 0 else banks[c]
                if ob3 == 2:
                    ns.sEv.inc(se.copy(ns.stg[0:64, (j % 2) * 512:((j % 2) + 1) * 512],
                                       src[0:64, 0:512] if ob3 % 2 == 0 else src[0:64, :]))
                else:
                    ns.sEv.inc(se.copy(ns.stg[:, (j % 2) * 512:((j % 2) + 1) * 512], src))

    def s5(sy):
        wait_pre(ns, sy, pre4)
        for ob3 in range(3):
            for c in range(CH):
                j = ob3 * 4 + c
                sy.wait_ge(ns.sEv.sem, ev40 + j + 1)
                if ob3 == 2:
                    ns.pSP.issue(sy, ns.ckv_scr[0:64, 4096 + c * 512: 4096 + (c + 1) * 512],
                                 ns.stg[0:64, (j % 2) * 512:((j % 2) + 1) * 512])
                else:
                    ns.pSP.issue(sy, ns.ckv_scr[:, ob3 * 2048 + c * 512: ob3 * 2048 + (c + 1) * 512],
                                 ns.stg[:, (j % 2) * 512:((j % 2) + 1) * 512])

    blk.gpsimd(g4)
    blk.tensor(t4)
    blk.sync(s5)
    blk.scalar(sc5)


def emit_p35(ns, blk):
    pre = snap(ns)
    A, Ff = ns.A, ns.Ff
    a0 = ns.sAh.n

    def s(sy):
        wait_pre(ns, sy, pre)
        ns.sAh.inc(sy.dma_start(out=A[:, A_CK:A_CK + 4096], in_=ns.ckv_scr[:, 0:4096]), 16)
        ns.sAh.inc(sy.dma_start(out=A[0:64, A_KP:A_KP + 2048], in_=ns.ckv_scr[0:64, 4096:6144]), 16)
        ns.sAh.inc(sy.dma_start(out=A[0:64, A_COS:A_COS + 4096].bitcast(F32),
                               in_=ns.dr["cos_t"]), 16)
        ns.sAh.inc(sy.dma_start(out=A[0:64, A_SIN:A_SIN + 4096].bitcast(F32),
                               in_=ns.dr["sin_t"]), 16)
        ns.sAh.inc(sy.dma_start(out=A[64:128, A_COS2:A_COS2 + 4096].bitcast(F32),
                               in_=ns.dr["cos_t"]), 16)
        ns.sAh.inc(sy.dma_start(out=A[64:128, A_SIN2:A_SIN2 + 4096].bitcast(F32),
                               in_=ns.dr["sin_t"]), 16)

    sq0, sqf0, st0 = ns.sSq.n, ns.sSqf.n, ns.sStat.n
    ve0, fin0, bc0, bce0, ya0, mm0 = (ns.sVe.n, ns.sFin.n, ns.sBc.n, ns.sBce.n,
                                      ns.sYa.n, ns.sMM.n)
    cosv = A[0:64, A_COS:A_COS + 4096].bitcast(F32)
    sinv = A[0:64, A_SIN:A_SIN + 4096].bitcast(F32)

    def v(ve):
        wait_pre(ns, ve, pre)
        ve.wait_ge(ns.sAh.sem, a0 + 16)
        for k2 in range(2):
            if k2 == 1:
                ve.wait_ge(ns.sSqf.sem, sqf0 + 1)
            ns.sSq.inc(ve.tensor_tensor(Ff[:, 0:2048],
                                        A[:, A_CK + k2 * 2048:A_CK + (k2 + 1) * 2048],
                                        A[:, A_CK + k2 * 2048:A_CK + (k2 + 1) * 2048],
                                        op=AluOpType.mult))

    def t(te):
        wait_pre(ns, te, pre)
        for k2 in range(2):
            te.wait_ge(ns.sSq.sem, sq0 + k2 + 1)
            for c in range(CH):
                i = te.matmul(ns.scores[0:1, c * 512:(c + 1) * 512], ns.ones_bf[:],
                              Ff[:, c * 512:(c + 1) * 512],
                              start=(k2 == 0), stop=(k2 == 1))
            if k2 == 0:
                ns.sSqf.inc(i)
            else:
                ns.sStat.inc(i)
        te.wait_ge(ns.sAh.sem, a0 + 32)
        for c in range(CH):
            te.wait_ge(ns.sFin.sem, fin0 + c + 1)
            if c > 0:
                te.wait_ge(ns.sBce.sem, bce0 + c)
            ns.sBc.inc(te.matmul(ns.pbA[:], ns.ones_f32[:], ns.stat_r[:], start=True, stop=True))
            if c > 0:
                te.wait_ge(ns.sVs.sem, ns._p35_vs0 + c)
            ns.sMM.inc(te.matmul(ns.pbD[0:64, 0:512], ns.p64_bf[:],
                                 A[0:64, A_KP + c * 512:A_KP + (c + 1) * 512],
                                 start=True, stop=True))

    def sc(se):
        wait_pre(ns, se, pre)
        for c in range(CH):
            se.wait_ge(ns.sVe.sem, ve0 + c + 1)
            if c > 0:
                se.wait_ge(ns.sBc.sem, bc0 + c)
            se.activation(ns.row2, ns.row1, AF.Ln, bias=0.0, scale=1.0)
            ns.sFin.inc(se.activation(ns.stat_r[:], ns.row2, AF.Exp, bias=0.0, scale=-0.5))
            se.wait_ge(ns.sBc.sem, bc0 + c + 1)
            if c > 0:
                se.wait_ge(ns.sYa.sem, ya0 + c)
            ns.sBce.inc(se.copy(ns.mrbf[:, 1024:1536], ns.pbA[:]))

    ns._p35_vs0 = ns.sVs.n

    def v2(ve):
        ve.wait_ge(ns.sAh.sem, a0 + 96)
        for c in range(CH):
            cs = slice(c * 512, (c + 1) * 512)
            if c == 0:
                ve.wait_ge(ns.sStat.sem, st0 + 1)
            else:
                ve.wait_ge(ns.sBc.sem, bc0 + c)
            ve.tensor_scalar_mul(ns.row1, ns.scores[0:1, cs], 1.0 / 256.0)
            ns.sVe.inc(ve.tensor_scalar(ns.row1, ns.row1, EPS_RMS, 1.0,
                                        op0=AluOpType.add, op1=AluOpType.mult))
            ve.wait_ge(ns.sMM.sem, mm0 + c + 1)
            ve.tensor_tensor(Ff[0:64, 2560:3072], A[0:64, A_KP + c * 512:A_KP + (c + 1) * 512],
                             cosv[0:64, c * 512:(c + 1) * 512], op=AluOpType.mult)
            ve.tensor_tensor(Ff[0:64, 3072:3584], ns.pbD[0:64, 0:512],
                             sinv[0:64, c * 512:(c + 1) * 512], op=AluOpType.mult)
            ns.sVs.inc(ve.tensor_tensor(A[0:64, A_KP + c * 512:A_KP + (c + 1) * 512],
                                        Ff[0:64, 2560:3072], Ff[0:64, 3072:3584],
                                        op=AluOpType.add))
            ve.wait_ge(ns.sBce.sem, bce0 + c + 1)
            i = None
            for k2 in range(2):
                off = A_CK + k2 * 2048 + c * 512
                ve.tensor_scalar(Ff[:, 2048:2560], A[:, off:off + 512],
                                 ns.gkv_sb[:, k2:k2 + 1], None, op0=AluOpType.mult)
                i = ve.tensor_tensor(A[:, off:off + 512], Ff[:, 2048:2560],
                                     ns.mrbf[:, 1024:1536], op=AluOpType.mult)
            ns.sYa.inc(i)

    blk.sync(s)
    blk.vector(v)
    blk.tensor(t)
    blk.scalar(sc)
    blk.vector(v2)


def _cg(i):
    return sum((j + 4) // 4 for j in range(i + 1))


def emit_heads(ns, blk):
    pre = snap(ns)
    A, Dn, Cf, Ff = ns.A, ns.Dn, ns.Cf, ns.Ff
    pw0, pctx0 = len(ns.pW.rec), len(ns.pCtx.rec)
    mm0, mm20, ev0, sc0, ve0, vs0, res0 = (ns.sMM.n, ns.sMM2.n, ns.sEv.n,
                                           ns.sSc.n, ns.sVe.n, ns.sVs.n,
                                           ns.sRES.n)
    D_MM, D_MM2, D_EV, D_SC, D_VE, D_VS, D_RES = 32, 44, 52, 16, 20, 48, 4

    mm_q = lambda c: c + 1
    mm_k = lambda c: 4 + c + 1
    mm_v = lambda tg: 8 + tg + 1
    mm_seq = {}
    k = 12
    for i in range(NT):
        k += 1
        mm_seq[("s", i)] = k
        if i >= 4 and (i - 1) % 4 == 3:
            pass
    mm_seq = {}
    k = 12
    emit_order = []
    for i in range(NT):
        k += 1
        mm_seq[("s", i)] = k
        emit_order.append(("s", i))
        if i >= 1 and (i - 1) % 4 == 3:
            J = (i - 1) // 4
            k += 1
            mm_seq[("pv", J)] = k
            emit_order.append(("pv", J))
    k += 1
    mm_seq[("pv", 3)] = k
    assert k == D_MM
    mm2_rot = lambda c: c + 1
    ev_q = lambda c: c + 1
    ev_k = lambda c: 4 + c + 1
    ev_v = lambda tg: 8 + tg + 1
    ev_g = lambda g: 12 + g + 1
    ev_pt_tile = lambda i: 12 + _cg(i)
    sc_exp = lambda i: i + 1
    ve_rope = lambda c: c + 1
    ve_nrm = lambda i: 4 + i + 1
    vs_msk = lambda i: 3 * i + 1
    vs_max = lambda i: 3 * i + 2
    vs_rcp = lambda i: 3 * i + 3

    def B(h, base, delta):
        return base + h * delta

    def g(gp):
        wait_pre(ns, gp, pre)
        for h in range(H):
            if h >= 2:
                gp.wait_ge(ns.sMM.sem, B(h - 2, mm0, D_MM) + 12)
            par = h % 2
            ns.pW.issue(gp,
                        [A[:, A_WQB + par * 1024:A_WQB + (par + 1) * 1024],
                         A[:, A_KVK + par * 128:A_KVK + (par + 1) * 128],
                         A[:, A_KVV + par * 256:A_KVV + (par + 1) * 256]],
                        [ns.dr["fw_qb"][h], ns.dr["fw_kvk"][h], ns.dr["fw_kvv"][h]], k=3)

    def t(te):
        wait_pre(ns, te, pre)
        for h in range(H):
            par = h % 2
            hb_mm, hb_mm2 = B(h, mm0, D_MM), B(h, mm20, D_MM2)
            hb_ev, hb_sc, hb_ve = B(h, ev0, D_EV), B(h, sc0, D_SC), B(h, ve0, D_VE)
            hb_res = B(h, res0, D_RES)
            ns.pW.wait(te, pw0 + h)
            for c in range(CH):
                te.wait_ge(ns.sEv.sem, hb_ev + c)
                for kk in range(8):
                    i = te.matmul(ns.pbA[:, 0:512],
                                  A[:, A_WQB + par * 1024 + kk * 128:A_WQB + par * 1024 + (kk + 1) * 128],
                                  Dn[:, kk * 2048 + c * 512: kk * 2048 + (c + 1) * 512],
                                  start=(kk == 0), stop=(kk == 7))
                ns.sMM.inc(i)
            for c in range(CH):
                te.wait_ge(ns.sEv.sem, hb_ev + ev_q(c))
                if c > 0:
                    te.wait_ge(ns.sVe.sem, hb_ve + ve_rope(c - 1))
                elif h > 0:
                    te.wait_ge(ns.sVe.sem, B(h - 1, ve0, D_VE) + 4)
                ns.sMM2.inc(te.matmul(ns.pbD[0:64, 0:512], ns.p64_hi[64:128, :],
                                      A[64:128, A_QF + c * 512:A_QF + (c + 1) * 512],
                                      start=True, stop=True))
            for c in range(CH):
                if c > 0:
                    te.wait_ge(ns.sEv.sem, hb_ev + ev_k(c - 1))
                for k2 in range(2):
                    te.matmul(ns.pbB[0:64, 0:512],
                              A[:, A_KVK + par * 128 + k2 * 64:A_KVK + par * 128 + (k2 + 1) * 64],
                              A[:, A_CK + k2 * 2048 + c * 512: A_CK + k2 * 2048 + (c + 1) * 512],
                              start=(k2 == 0), stop=(k2 == 1))
                ns.sMM.inc(te.matmul(ns.pbB[64:128, 0:512], ns.ident_bf[0:64, 0:64],
                                     A[0:64, A_KP + c * 512:A_KP + (c + 1) * 512],
                                     start=True, stop=True))
            for tg in range(4):
                if tg > 0:
                    te.wait_ge(ns.sEv.sem, hb_ev + ev_v(tg - 1))
                for tv in range(4):
                    tt = tg * 4 + tv
                    for k2 in range(2):
                        i = te.matmul(ns.pbC[:, tv * 128:(tv + 1) * 128],
                                      A[:, A_CK + k2 * 2048 + tt * 128: A_CK + k2 * 2048 + (tt + 1) * 128],
                                      A[:, A_KVV + par * 256 + k2 * 128:A_KVV + par * 256 + (k2 + 1) * 128],
                                      start=(k2 == 0), stop=(k2 == 1))
                ns.sMM.inc(i)
            te.wait_ge(ns.sVe.sem, hb_ve + 4)
            te.wait_ge(ns.sEv.sem, hb_ev + 12)

            def emit_tp(i):
                te.wait_ge(ns.sVe.sem, hb_ve + ve_nrm(i))
                g_lo = _cg(i - 1) if i > 0 else 0
                ng = (i + 4) // 4
                for gi in range(ng):
                    gg = g_lo + gi
                    pb = ns.pbA if gg % 2 == 0 else ns.pbB
                    if gg >= 2:
                        te.wait_ge(ns.sEv.sem, hb_ev + ev_g(gg - 2))
                    m = min(4, i + 1 - 4 * gi)
                    for mi in range(m):
                        tau = 4 * gi + mi
                        mmi = te.matmul(pb[:, mi * 128:(mi + 1) * 128],
                                        A[:, A_PR + tau * 128:A_PR + (tau + 1) * 128],
                                        ns.ident_bf[:], start=True, stop=True)
                    ns.sMM2.inc(mmi)

            def emit_pv(J):
                te.wait_ge(ns.sEv.sem, hb_ev + ev_pt_tile(4 * J + 3))
                if h > 0 or J > 0:
                    te.wait_ge(ns.sRES.sem, hb_res + J)
                for tau in range(4 * J + 4):
                    c0 = max(0, tau - 4 * J) * 128
                    i = te.matmul(ns.pbC[:, c0:512],
                                  A[:, A_V + tau * 128:A_V + (tau + 1) * 128],
                                  A[:, A_PT + tau * 512 + c0:A_PT + (tau + 1) * 512],
                                  start=(tau == 0), stop=(tau == 4 * J + 3))
                ns.sMM.inc(i)

            for i in range(NT):
                L = (i + 1) * 128
                if h > 0 or i > 0:
                    te.wait_ge(ns.sSc.sem, hb_sc + i)
                for kc in range(0, L, 512):
                    N = min(512, L - kc)
                    mmi = te.matmul(ns.scores[:, kc:kc + N],
                                    A[:, A_QF + i * 128:A_QF + (i + 1) * 128],
                                    A[:, A_KF + kc:A_KF + kc + N], start=True, stop=True)
                ns.sMM.inc(mmi)
                if i >= 1:
                    emit_tp(i - 1)
                    if (i - 1) % 4 == 3:
                        emit_pv((i - 1) // 4)
            emit_tp(NT - 1)
            emit_pv(3)

    def sc(se):
        wait_pre(ns, se, pre)
        for h in range(H):
            hb_mm, hb_mm2 = B(h, mm0, D_MM), B(h, mm20, D_MM2)
            hb_ev, hb_sc, hb_vs = B(h, ev0, D_EV), B(h, sc0, D_SC), B(h, vs0, D_VS)
            for c in range(CH):
                se.wait_ge(ns.sMM.sem, hb_mm + mm_q(c))
                ns.sEv.inc(se.copy(A[:, A_QF + c * 512:A_QF + (c + 1) * 512], ns.pbA[:, 0:512]))
            for c in range(CH):
                se.wait_ge(ns.sMM.sem, hb_mm + mm_k(c))
                ns.sEv.inc(se.copy(A[:, A_KF + c * 512:A_KF + (c + 1) * 512], ns.pbB[:, 0:512]))
            for tg in range(4):
                se.wait_ge(ns.sMM.sem, hb_mm + mm_v(tg))
                ns.sEv.inc(se.copy(A[:, A_V + tg * 512:A_V + (tg + 1) * 512], ns.pbC[:, 0:512]))
            for i in range(NT):
                L = (i + 1) * 128
                se.wait_ge(ns.sMM.sem, hb_mm + mm_seq[("s", i)])
                se.wait_ge(ns.sVs.sem, hb_vs + vs_max(i))
                if i > 0:
                    se.wait_ge(ns.sMM2.sem, hb_mm2 + 4 + _cg(i - 1))
                ns.sSc.inc(se.activation(A[:, A_PR:A_PR + L], ns.scores[:, 0:L], AF.Exp,
                                         bias=ns.nm_sb[:, 0:1], scale=1.0,
                                         accum_out=ns.den_sb[:, 0:1]))
                J = i // 4
                cb = i - 4 * J
                g_lo = _cg(i - 1) if i > 0 else 0
                ng = (i + 4) // 4
                if J > 0:
                    se.wait_ge(ns.sMM.sem, hb_mm + mm_seq[("pv", J - 1)])
                for gi in range(ng):
                    gg = g_lo + gi
                    pb = ns.pbA if gg % 2 == 0 else ns.pbB
                    se.wait_ge(ns.sMM2.sem, hb_mm2 + 4 + gg + 1)
                    m = min(4, i + 1 - 4 * gi)
                    base = A_PT + 4 * gi * 512 + cb * 128
                    if m > 1:
                        dst = A[:, base:base + m * 512].rearrange("p (t x) -> p t x", x=512)[:, :, 0:128]
                        src = pb[:, 0:m * 128].rearrange("p (t x) -> p t x", x=128)
                    else:
                        dst = A[:, base:base + 128]
                        src = pb[:, 0:128]
                    ns.sEv.inc(se.copy(dst, src))

    def v(ve):
        wait_pre(ns, ve, pre)
        cos2 = A[64:128, A_COS2:A_COS2 + 4096].bitcast(F32)
        sin2 = A[64:128, A_SIN2:A_SIN2 + 4096].bitcast(F32)
        for h in range(H):
            hb_mm, hb_mm2 = B(h, mm0, D_MM), B(h, mm20, D_MM2)
            hb_ev, hb_sc, hb_vs = B(h, ev0, D_EV), B(h, sc0, D_SC), B(h, vs0, D_VS)
            hb_res = B(h, res0, D_RES)
            for c in range(CH):
                ve.wait_ge(ns.sMM2.sem, hb_mm2 + mm2_rot(c))
                ve.wait_ge(ns.sEv.sem, hb_ev + ev_q(c))
                ve.tensor_tensor(Ff[0:64, 0:512], A[64:128, A_QF + c * 512:A_QF + (c + 1) * 512],
                                 cos2[0:64, c * 512:(c + 1) * 512], op=AluOpType.mult)
                ve.tensor_tensor(Ff[0:64, 512:1024], ns.pbD[0:64, 0:512],
                                 sin2[0:64, c * 512:(c + 1) * 512], op=AluOpType.mult)
                ns.sVe.inc(ve.tensor_tensor(A[64:128, A_QF + c * 512:A_QF + (c + 1) * 512],
                                            Ff[0:64, 0:512], Ff[0:64, 512:1024],
                                            op=AluOpType.add))
            for i in range(NT):
                L = (i + 1) * 128
                ve.wait_ge(ns.sMM.sem, hb_mm + mm_seq[("s", i)])
                ns.sVs.inc(ve.tensor_tensor(ns.scores[:, i * 128:(i + 1) * 128],
                                            ns.scores[:, i * 128:(i + 1) * 128], ns.tri_sb[:],
                                            op=AluOpType.add))
                ve.wait_ge(ns.sVs.sem, hb_vs + vs_msk(i))
                ns.sVs.inc(ve.reduce_max(ns.nm_sb[:], ns.scores[:, 0:L], axis=AX, negate=True))
                ve.wait_ge(ns.sSc.sem, hb_sc + sc_exp(i))
                ns.sVs.inc(ve.reciprocal(ns.rden_sb[:], ns.den_sb[:]))
                ve.wait_ge(ns.sVs.sem, hb_vs + vs_rcp(i))
                ns.sVe.inc(ve.tensor_scalar_mul(A[:, A_PR:A_PR + L], A[:, A_PR:A_PR + L],
                                                ns.rden_sb[:, 0:1]))
                if i % 4 == 3:
                    J = i // 4
                    gJ = h * 4 + J
                    ve.wait_ge(ns.sMM.sem, hb_mm + mm_seq[("pv", J)])
                    ns.pCtx.wait(ve, pctx0 + gJ - 2)
                    ns.sRES.inc(ve.tensor_copy(
                        ns.ctxstg[:, (gJ % 2) * 512:((gJ % 2) + 1) * 512], ns.pbC[:, 0:512]))

    def s(sy):
        wait_pre(ns, sy, pre)
        for h in range(H):
            hb_res = B(h, res0, D_RES)
            for J in range(4):
                gJ = h * 4 + J
                sy.wait_ge(ns.sRES.sem, hb_res + J + 1)
                ns.pCtx.issue(sy, ns.ctx_scr[h, :, J * 512:(J + 1) * 512],
                              ns.ctxstg[:, (gJ % 2) * 512:((gJ % 2) + 1) * 512])

    blk.gpsimd(g)
    blk.tensor(t)
    blk.scalar(sc)
    blk.sync(s)
    blk.vector(v)
    assert ns.sMM.n == mm0 + H * D_MM, (ns.sMM.n - mm0, H * D_MM)
    assert ns.sMM2.n == mm20 + H * D_MM2, (ns.sMM2.n - mm20, H * D_MM2)
    assert ns.sEv.n == ev0 + H * D_EV, (ns.sEv.n - ev0, H * D_EV)
    assert ns.sSc.n == sc0 + H * D_SC
    assert ns.sVe.n == ve0 + H * D_VE
    assert ns.sVs.n == vs0 + H * D_VS
    assert ns.sRES.n == res0 + H * D_RES


def emit_p5(ns, blk):
    pre = snap(ns)
    A, Cf, Ff = ns.A, ns.Cf, ns.Ff
    a0 = ns.sAh.n

    def s0(sy):
        wait_pre(ns, sy, pre)
        for j in range(H):
            ns.sAh.inc(sy.dma_start(out=A[:, j * 2048:(j + 1) * 2048], in_=ns.ctx_scr[j]), 16)

    blk.sync(s0)
    pre2 = snap(ns)
    mm0, res0 = ns.sMM.n, ns.sRES.n
    pw0, px0, pxw0 = len(ns.pW.rec), len(ns.pX.rec), len(ns.pXW.rec)

    def g(gp):
        wait_pre(ns, gp, pre2)
        for ob in range(DBLK):
            for half in range(2):
                si = 2 * ob + half
                if ob >= 1:
                    gp.wait_ge(ns.sMM.sem, mm0 + 4 * ob)
                ns.pW.issue(gp, Ff[:, (si % 2) * 2048:((si % 2) + 1) * 2048],
                            ns.dr["fw_wo"][ob, :, half * 2048:(half + 1) * 2048])

    def s(sy):
        wait_pre(ns, sy, pre2)
        for ob in range(DBLK):
            for c in range(CH):
                xi = ob * CH + c
                if xi >= 2:
                    sy.wait_ge(ns.sRES.sem, res0 + xi - 1)
                ns.pX.issue(sy, Cf[:, C_XF + (xi % 2) * 512: C_XF + ((xi % 2) + 1) * 512],
                            ns.x1t[ob * 128:(ob + 1) * 128, c * 512:(c + 1) * 512])
            for c in range(CH):
                xi = ob * CH + c
                sy.wait_ge(ns.sRES.sem, res0 + xi + 1)
                ns.pXW.issue(sy, ns.x2t[ob * 128:(ob + 1) * 128, c * 512:(c + 1) * 512],
                             Cf[:, C_X1 + (xi % 2) * 512: C_X1 + ((xi % 2) + 1) * 512])

    def t(te):
        wait_pre(ns, te, pre2)
        te.wait_ge(ns.sAh.sem, a0 + 16 * H)
        for ob in range(DBLK):
            banks = ns.pb if ob % 2 == 0 else [ns.scores[:, i * 512:(i + 1) * 512] for i in range(4)]
            if ob >= 2:
                te.wait_ge(ns.sRES.sem, res0 + (ob - 2) * CH + CH)
            for half in range(2):
                si = 2 * ob + half
                ns.pW.wait(te, pw0 + si)
                for c in range(CH):
                    for kk in range(16):
                        jglob = half * 16 + kk
                        i = te.matmul(banks[c][:, 0:512] if ob % 2 == 0 else banks[c],
                                      Ff[:, (si % 2) * 2048 + kk * 128:(si % 2) * 2048 + (kk + 1) * 128],
                                      A[:, jglob * 2048 + c * 512: jglob * 2048 + c * 512 + 512],
                                      start=(half == 0 and kk == 0), stop=(half == 1 and kk == 15))
                    if half == 1:
                        ns.sMM.inc(i)

    def v(ve):
        wait_pre(ns, ve, pre2)
        for ob in range(DBLK):
            banks = ns.pb if ob % 2 == 0 else [ns.scores[:, i * 512:(i + 1) * 512] for i in range(4)]
            for c in range(CH):
                xi = ob * CH + c
                ve.wait_ge(ns.sMM.sem, mm0 + ob * CH + c + 1)
                ns.pX.wait(ve, px0 + xi)
                ns.pXW.wait(ve, pxw0 + xi - 2)
                src = banks[c][:, 0:512] if ob % 2 == 0 else banks[c]
                ns.sRES.inc(ve.tensor_tensor(
                    Cf[:, C_X1 + (xi % 2) * 512: C_X1 + ((xi % 2) + 1) * 512],
                    Cf[:, C_XF + (xi % 2) * 512: C_XF + ((xi % 2) + 1) * 512],
                    src, op=AluOpType.add))

    blk.gpsimd(g)
    blk.sync(s)
    blk.tensor(t)
    blk.vector(v)


def emit_final(ns, blk):
    pre = snap(ns)

    def s(sy):
        wait_pre(ns, sy, pre)

    blk.sync(s)


def full_program(nc, es, taps=(), dev_expose=False):
    ns = build(nc, es, debug_taps=taps, dev_expose=dev_expose)
    blk = nc.Block()
    with blk:
        emit_consts(ns, blk)
        emit_stats(ns, blk, ns.x0t)
        ya = emit_apply(ns, blk, 0.5)
        if "y0" in ns.dbg:
            _tap_big(ns, blk, ns.A, ns.dbg["y0"])
        emit_ffn(ns, blk, ns.x0t, ns.x1t, ya["ya"], use_basebf=False, cache_basebf=True,
                 tap=("midg" if "midg" in ns.dbg else ("gmap" if "gmap" in ns.dbg else None)))
        emit_stats(ns, blk, ns.x1t)
        ya2 = emit_apply(ns, blk, 1.0)
        emit_attnprep(ns, blk, ya2["ya"])
        if "cq" in ns.dbg:
            _tap_cq(ns, blk)
        emit_p35(ns, blk)
        if "ck" in ns.dbg:
            _tap_bf16(ns, blk, ns.A, A_CK, ns.dbg["ck"], 4096)
        if "kp" in ns.dbg:
            _tap_bf16(ns, blk, ns.A, A_KP, ns.dbg["kp"], 2048, rows=64)
        emit_heads(ns, blk)
        if "qh" in ns.dbg:
            _tap_bf16(ns, blk, ns.A, A_QF, ns.dbg["qh"], 2048)
        if "kh" in ns.dbg:
            _tap_bf16(ns, blk, ns.A, A_KF, ns.dbg["kh"], 2048)
        if "vh" in ns.dbg:
            _tap_bf16(ns, blk, ns.A, A_V, ns.dbg["vh"], 2048)
        if "probs" in ns.dbg:
            _tap_bf16(ns, blk, ns.A, A_PR, ns.dbg["probs"], 2048)
        if "ctxh" in ns.dbg:
            _tap_ctx(ns, blk)
        emit_p5(ns, blk)
        emit_stats(ns, blk, ns.x2t)
        if "y2" in ns.dbg:
            pass
        ya3 = emit_apply(ns, blk, 0.5)
        if "y2" in ns.dbg:
            _tap_big(ns, blk, ns.A, ns.dbg["y2"])
        emit_ffn(ns, blk, ns.x2t, ns.x3t, ya3["ya"], use_basebf=True, cache_basebf=False)
        emit_final(ns, blk)
    return ns


def _tap_big(ns, blk, buf, dst):
    pre = snap(ns)

    def g(gp):
        wait_pre(ns, gp, pre)
        ns.sO.inc(gp.dma_start(
            out=dst.rearrange("(d p) t -> p d t", p=128),
            in_=buf[:, 0:65536].rearrange("p (d t) -> p d t", d=32)), 16)

    blk.gpsimd(g)
    _tap_barrier(ns, blk)


def _tap_cq(ns, blk):
    pre = snap(ns)

    def g(gp):
        wait_pre(ns, gp, pre)
        ns.sO.inc(gp.dma_start(
            out=ns.dbg["cq"].rearrange("(d p) t -> p d t", p=128),
            in_=ns.Dn[:, 0:16384].rearrange("p (d t) -> p d t", d=8)), 16)

    blk.gpsimd(g)
    _tap_barrier(ns, blk)


def _tap_ctx(ns, blk):
    pre = snap(ns)

    def s(sy):
        wait_pre(ns, sy, pre)
        ns.sA.inc(sy.dma_start(out=ns.stg[:, 0:512], in_=ns.ctx_scr[0, :, 0:512]), 16)
        ns.sA.inc(sy.dma_start(out=ns.mrbf[:, 0:512], in_=ns.ctx_scr[0, :, 512:1024]), 16)
        ns.sA.inc(sy.dma_start(out=ns.mrbf[:, 512:1024], in_=ns.ctx_scr[0, :, 1024:1536]), 16)
        ns.sA.inc(sy.dma_start(out=ns.mrbf[:, 1024:1536], in_=ns.ctx_scr[0, :, 1536:2048]), 16)

    def g(gp):
        a_t = ns.sA.n

        def _g(gp):
            gp.wait_ge(ns.sA.sem, a_t)
            ns.sO.inc(gp.dma_start(out=ns.dbg["ctxh"][:, 0:512], in_=ns.stg[:, 0:512]), 16)
            ns.sO.inc(gp.dma_start(out=ns.dbg["ctxh"][:, 512:1024], in_=ns.mrbf[:, 0:512]), 16)
            ns.sO.inc(gp.dma_start(out=ns.dbg["ctxh"][:, 1024:1536], in_=ns.mrbf[:, 512:1024]), 16)
            ns.sO.inc(gp.dma_start(out=ns.dbg["ctxh"][:, 1536:2048], in_=ns.mrbf[:, 1024:1536]), 16)
        return _g

    blk.sync(s)
    blk.gpsimd(g(None))
    _tap_barrier(ns, blk)


def _tap_bf16(ns, blk, buf, off, dst_dram, width, rows=128):
    pre = snap(ns)

    def g(gp):
        wait_pre(ns, gp, pre)
        ns.sO.inc(gp.dma_start(out=dst_dram, in_=buf[0:rows, off:off + width]), 16)

    blk.gpsimd(g)
    _tap_barrier(ns, blk)


def _tap_barrier(ns, blk):
    pre = snap(ns)
    for name in ["gpsimd", "sync", "tensor", "scalar", "vector"]:
        getattr(blk, name)(lambda e: wait_pre(ns, e, pre))


def deint_perm():
    return np.array([2 * i for i in range(32)] + [2 * i + 1 for i in range(32)])


def rot_matrix():
    P = np.zeros((64, 64), dtype=np.float32)
    for i in range(32):
        P[i, i + 32] = -1.0
        P[i + 32, i] = 1.0
    return P


def yarn_cos_sin(seq_len, dim=64, base=10000.0, factor=128.0, beta_fast=32.0,
                 beta_slow=1.0, orig_max=8192, mscale=1.0, mscale_all_dim=1.0):
    def corr_dim(r):
        return dim * math.log(orig_max / (r * 2 * math.pi)) / (2 * math.log(base))
    low = max(math.floor(corr_dim(beta_fast)), 0)
    high = min(math.ceil(corr_dim(beta_slow)), dim - 1)
    hi = high + 0.001 if low == high else float(high)
    ramp = np.clip((np.arange(dim // 2, dtype=np.float32) - low) / (hi - low), 0.0, 1.0)
    inv_freq_mask = 1.0 - ramp
    freq_extra = 1.0 / base ** (np.arange(0, dim, 2, dtype=np.float32) / dim)
    freq_inter = freq_extra / factor
    inv_freq = freq_inter * (1.0 - inv_freq_mask) + freq_extra * inv_freq_mask
    t = np.arange(seq_len, dtype=np.float32)
    freqs = np.outer(t, inv_freq)
    emb = np.concatenate([freqs, freqs], axis=-1)

    def gm(s, m):
        return 1.0 if s <= 1 else 0.1 * m * math.log(s) + 1.0
    ms = gm(factor, mscale) / gm(factor, mscale_all_dim)
    return (np.cos(emb) * ms).astype(np.float32), (np.sin(emb) * ms).astype(np.float32)


def lhsT_tiles(WT, mblocks):
    K, M = WT.shape
    assert K % 128 == 0
    kb = K // 128
    out = []
    off = 0
    for w in mblocks:
        blkk = WT[:, off:off + w].reshape(kb, 128, w).transpose(1, 0, 2).reshape(128, kb * w)
        out.append(blkk)
        off += w
    assert off == M
    return np.ascontiguousarray(np.stack(out, 0).astype(np.float32))


def host_prep(weights):
    pi = deint_perm()
    out = {}
    out["fw_base"] = lhsT_tiles(np.ascontiguousarray(weights["L0_ffn_base_W"].T), [128] * 32)
    out["fw_qa"] = lhsT_tiles(np.ascontiguousarray(weights["L0_attn_wq_a"].T), [128] * 8)
    wqbT = weights["L0_attn_wq_b"].T
    heads = []
    for h in range(H):
        cols = [h * 128 + i for i in range(64)] + [h * 128 + 64 + int(pi[i]) for i in range(64)]
        heads.append(lhsT_tiles(np.ascontiguousarray(wqbT[:, cols]), [128])[0])
    out["fw_qb"] = np.ascontiguousarray(np.stack(heads, 0))
    wkvaT = np.ascontiguousarray(weights["L0_attn_wkv_a"].T)
    perm = list(range(256)) + [256 + int(pi[i]) for i in range(64)]
    wkvaT = wkvaT[:, perm]
    pad = np.zeros((4096, 384), dtype=np.float32)
    pad[:, :320] = wkvaT
    out["fw_kva"] = lhsT_tiles(pad, [128, 128, 128])
    wkvbT = weights["L0_attn_wkv_b"].T
    kvk, kvv = [], []
    for h in range(H):
        kc = np.ascontiguousarray(wkvbT[:, [h * 192 + i for i in range(64)]])
        vc = np.ascontiguousarray(wkvbT[:, [h * 192 + 64 + i for i in range(128)]])
        kvk.append(kc.reshape(2, 128, 64).transpose(1, 0, 2).reshape(128, 128))
        kvv.append(vc.reshape(2, 128, 128).transpose(1, 0, 2).reshape(128, 256))
    out["fw_kvk"] = np.ascontiguousarray(np.stack(kvk, 0).astype(np.float32))
    out["fw_kvv"] = np.ascontiguousarray(np.stack(kvv, 0).astype(np.float32))
    out["fw_wo"] = lhsT_tiles(np.ascontiguousarray(weights["L0_attn_wo"].T), [128] * 32)
    out["fw_rt"] = lhsT_tiles(np.ascontiguousarray(weights["L0_ffn_router_W"].T), [8])[0]
    lA = weights["L0_ffn_lora_A"].transpose(1, 0, 2).reshape(4096, 128)
    out["fw_la"] = lhsT_tiles(np.ascontiguousarray(lA), [128])[0]
    out["fw_lb"] = np.ascontiguousarray(weights["L0_ffn_lora_B"].reshape(128, 4096).astype(np.float32))
    out["gq"] = np.ascontiguousarray(weights["L0_attn_q_a_ln_w"].reshape(8, 128).T.astype(np.float32))
    out["gkv"] = np.ascontiguousarray(weights["L0_attn_kv_a_ln_w"].reshape(2, 128).T.astype(np.float32))
    cos, sin = yarn_cos_sin(S)
    out["cos_t"] = np.ascontiguousarray(cos.T)
    out["sin_t"] = np.ascontiguousarray(sin.T)
    out["ident"] = np.eye(128, dtype=np.float32)
    out["p64t"] = np.ascontiguousarray(rot_matrix().T)
    tri = np.zeros((128, 128), dtype=np.float32)
    for p in range(128):
        tri[p, p + 1:] = NEG_BIG
    out["tri"] = tri
    out["ones_row"] = np.ones((1, 128), dtype=np.float32)
    R2 = np.zeros((8, 128), dtype=np.float32)
    for e in range(8):
        R2[e, e * 16:(e + 1) * 16] = 2.0
    out["r2"] = R2
    return out


import concourse.bass_utils as _bass_utils


def kernel(**inputs):
    from contextlib import ExitStack
    x = np.asarray(inputs["L0_attn_x"], dtype=np.float32)
    B = x.shape[0]
    host = host_prep({k: np.asarray(v, dtype=np.float32) for k, v in inputs.items()})
    nc = bass.Bass("TRN2", target_bir_lowering=False)
    es = ExitStack()
    with es:
        full_program(nc, es)
    in_maps = []
    for b in range(B):
        m = dict(host)
        m["x0t"] = np.ascontiguousarray(x[b].T)
        in_maps.append(m)
    res = _bass_utils.run_bass_kernel_spmd(nc, in_maps, core_ids=list(range(B)))
    out = np.stack([np.ascontiguousarray(r["x3t"].T) for r in res.results], axis=0)
    return out
